# version 08
# baseline (speedup 1.0000x reference)
import contextlib
import numpy as np
import ml_dtypes
import concourse.bass as bass
import concourse.mybir as mybir
from concourse.bass_utils import run_bass_kernel_spmd

F32 = mybir.dt.float32
I32 = mybir.dt.int32
BF16 = mybir.dt.bfloat16
FP8 = mybir.dt.float8e4
AF = mybir.ActivationFunctionType
OP = mybir.AluOpType
BF = ml_dtypes.bfloat16

LN2 = float(np.log(2.0))
INVLN2 = float(1.0 / np.log(2.0))

B, S, D, H = 4, 4096, 1024, 4096
TOK = 2048
NT_X = S // 128
DCH = D // 128
HCH = H // 128
NQ8 = TOK // 256
SCALE = 1.0 / 32.0

C_TINY, C_M3P, C_M10P, C_CL3, C_CL10, C_M3N, C_M10N, C_EPS = range(8)


def build_nc(debug=False):
    nc = bass.Bass("TRN2", target_bir_lowering=False, detect_race_conditions=False)
    ap = {}

    def dI(name, shape, dt=F32):
        ap[name] = nc.dram_tensor(name, shape, dt, kind="ExternalInput").ap()
        return ap[name]

    x_d = dI("x", [S, D])
    wq_d = dI("wq", [D, D])
    dI("bqc", [128, DCH])
    w1t_d = dI("w1t", [D, H])
    dI("b1c", [128, HCH])
    w2t_d = dI("w2t", [H, D])
    dI("b2c", [128, 8])
    dI("cpool", [128, 8])
    dI("cbf", [128, 2], BF16)
    dI("onesrow", [1, 128])
    dI("negrow", [1, 128])
    out_d = nc.dram_tensor("out", [TOK, D], F32, kind="ExternalOutput").ap()
    spk = "ExternalOutput" if debug else "Internal"
    if debug:
        ap["dbg_stats"] = nc.dram_tensor("dbg_stats", [6, 128, NT_X], F32,
                                         kind="ExternalOutput").ap()
    sp_xnt = nc.dram_tensor("sp_xnt", [128, DCH * TOK], BF16, kind=spk).ap()
    sp_s = nc.dram_tensor("sp_s", [TOK, D], F32, kind=spk).ap()
    sp_w2q = nc.dram_tensor("sp_w2q", [H, D], FP8, kind=spk).ap()
    sp_b2 = nc.dram_tensor("sp_b2", [1024], F32, kind="Internal").ap()
    sp_z = nc.dram_tensor("sp_z", [NQ8, 256], F32, kind="Internal").ap()

    es = contextlib.ExitStack()
    sbt = lambda *a, **k: es.enter_context(nc.sbuf_tensor(*a, **k))
    SEM = lambda n: es.enter_context(nc.semaphore(n))

    cpool = sbt("cpool_s", [128, 8], F32)
    cbf = sbt("cbf_s", [128, 2], BF16)
    bqc = sbt("bqc_s", [128, DCH], F32)
    b1c = sbt("b1c_s", [128, HCH], F32)
    b2c = sbt("b2c_s", [128, 8], F32)
    b2qc = sbt("b2qc_s", [128, 8], F32)
    b2q_row = sbt("b2q_row", [1, D], F32)
    onesrow = sbt("ones_s", [1, 128], F32)
    negrow = sbt("neg_s", [1, 128], F32)
    c_row = sbt("c_row", [1, TOK], F32)
    z_sb = sbt("z_sb", [1, 256], F32)
    zcols = sbt("zcols", [128, 16], F32)
    invz = sbt("invz", [128, 16], F32)
    stats = {k: sbt("st_" + k, [128, NT_X], F32)
             for k in ("ssum", "ssq", "mu", "var", "sd", "r0", "t1", "t2")}
    q_u = sbt("q_u", [128, 512], F32)
    q_w = sbt("q_w", [128, 512], F32)
    q_n1 = sbt("q_n1", [128, 512], F32)
    q_v = sbt("q_v", [128, 512], F32)
    q_cm1 = sbt("q_cm1", [128, 512], F32)
    q_t = sbt("q_t", [128, 512], F32)
    q_ei = sbt("q_ei", [128, 512], I32)
    q_abs = sbt("q_abs", [128, 512], F32)
    q_L = [sbt(f"q_L{i}", [128, 512], F32) for i in range(2)]
    q_e2 = [sbt(f"q_e2_{i}", [128, 512], F32) for i in range(2)]
    q_si = [sbt(f"q_si{i}", [128, 512], F32) for i in range(2)]
    q_sc = [sbt(f"q_sc{i}", [128, 512], F32) for i in range(2)]

    sm = {}
    for nme in ("ldw sq var sd xn mmq qe sq2 mmc mmc2 ce spx tr2 "
                "b2a b2b b2c2 b2d b2r mms exp mmz ze zt zr mmpv se ss "
                "w2sa w2v1 w2sb w2v2 cxnt cw2 mmh hr hl he hs hq "
                "mmf fe oa oe os oq g cmb").split():
        sm[nme] = SEM("s_" + nme)
    sm["vfl"] = SEM("s_vfl")
    vfl_n = [0]

    def VF(v, ins):
        vfl_n[0] += 1
        ins.then_inc(sm["vfl"], 1)
        v.wait_ge(sm["vfl"], vfl_n[0])
        return ins

    for nme in "ldx tr gw w2l w2st cw1 cs cx outd".split():
        sm[nme] = [SEM("s_" + nme + "0"), SEM("s_" + nme + "1")]

    qb = {}

    def q_echain(v, kind, j, src_L, W=512):
        fl = (lambda ins: VF(v, ins)) if W < 512 else (lambda ins: ins)
        if kind == "h":
            u, ef0, w, n1, e2 = (qb["u"][:, :W], qb["ef0"][:, :W], qb["w"][:, :W],
                                 qb["n1"][:, :W], qb["e2"][j % 2][:, :W])
        else:
            u, ef0, w, n1 = q_u[:, :W], q_cm1[:, :W], q_w[:, :W], q_n1[:, :W]
            e2 = q_e2[j % 2][:, :W]
        ci = C_CL3 if kind in ("h", "w2") else C_CL10
        fl(v.tensor_scalar(out=u, in0=src_L, scalar1=INVLN2,
                           scalar2=cpool[:, ci:ci + 1], op0=OP.mult, op1=OP.max))
        fl(v.tensor_single_scalar(out=q_ei[:, :W], in_=u, scalar=0.0, op=OP.add))
        fl(v.tensor_single_scalar(out=ef0, in_=q_ei[:, :W], scalar=-1.0, op=OP.add))
        fl(v.tensor_tensor(out=w, in0=u, in1=ef0, op=OP.subtract))
        fl(v.scalar_tensor_tensor(out=n1, in0=w, scalar=1.0, in1=ef0,
                                  op0=OP.is_ge, op1=OP.add))
        return v.scalar_tensor_tensor(out=e2, in0=w, scalar=2.0, in1=n1,
                                      op0=OP.is_ge, op1=OP.add)

    def q_scales(s, kind, j, W=512, want_sc=True):
        e2 = (qb["e2"] if kind == "h" else q_e2)[j % 2][:, :W]
        mp = C_M3P if kind in ("h", "w2") else C_M10P
        mn = C_M3N if kind in ("h", "w2") else C_M10N
        r = s.activation(out=q_si[j % 2][:, :W], in_=e2, func=AF.Exp,
                         scale=-LN2, bias=cpool[:, mp:mp + 1])
        if want_sc:
            r = s.activation(out=q_sc[j % 2][:, :W], in_=e2, func=AF.Exp,
                             scale=LN2, bias=cpool[:, mn:mn + 1])
        return r

    def q_round(v, kind, j, src_x, dst_q, W=512):
        fl = (lambda ins: VF(v, ins)) if W < 512 else (lambda ins: ins)
        sinv = q_si[j % 2][:, :W]
        sc = q_sc[j % 2][:, :W]
        exact = kind in ("w2", "b2")
        if exact:
            fl(v.tensor_single_scalar(out=q_ei[:, :W], in_=sinv, scalar=0.0, op=OP.add))
            fl(v.tensor_single_scalar(out=q_cm1[:, :W], in_=q_ei[:, :W], scalar=-1.0, op=OP.add))
            fl(v.tensor_tensor(out=q_w[:, :W], in0=sinv, in1=q_cm1[:, :W], op=OP.subtract))
            fl(v.scalar_tensor_tensor(out=q_n1[:, :W], in0=q_w[:, :W], scalar=0.5,
                                      in1=q_cm1[:, :W], op0=OP.is_ge, op1=OP.add))
            fl(v.scalar_tensor_tensor(out=sinv, in0=q_w[:, :W], scalar=1.5,
                                      in1=q_n1[:, :W], op0=OP.is_ge, op1=OP.add))
            fl(v.reciprocal(out=sc, in_=sinv))
        fl(v.tensor_tensor(out=q_t[:, :W], in0=src_x, in1=sinv, op=OP.mult))
        fl(v.tensor_single_scalar(out=q_ei[:, :W], in_=q_t[:, :W], scalar=0.0, op=OP.add))
        fl(v.tensor_single_scalar(out=q_cm1[:, :W], in_=q_ei[:, :W], scalar=-1.0, op=OP.add))
        fl(v.tensor_tensor(out=q_v[:, :W], in0=q_t[:, :W], in1=q_cm1[:, :W], op=OP.subtract))
        fl(v.scalar_tensor_tensor(out=q_n1[:, :W], in0=q_v[:, :W], scalar=0.5,
                                  in1=q_cm1[:, :W], op0=OP.is_ge, op1=OP.add))
        fl(v.scalar_tensor_tensor(out=q_t[:, :W], in0=q_v[:, :W], scalar=1.5,
                                  in1=q_n1[:, :W], op0=OP.is_ge, op1=OP.add))
        n_ap = q_t[:, :W]
        if exact:
            fl(v.tensor_single_scalar(out=q_w[:, :W], in_=n_ap, scalar=0.5, op=OP.mult))
            fl(v.tensor_single_scalar(out=q_ei[:, :W], in_=q_w[:, :W], scalar=0.0, op=OP.add))
            fl(v.tensor_single_scalar(out=q_cm1[:, :W], in_=q_ei[:, :W], scalar=-1.0, op=OP.add))
            fl(v.tensor_tensor(out=q_u[:, :W], in0=q_w[:, :W], in1=q_cm1[:, :W], op=OP.subtract))
            fl(v.scalar_tensor_tensor(out=q_n1[:, :W], in0=q_u[:, :W], scalar=1.0,
                                      in1=q_cm1[:, :W], op0=OP.is_ge, op1=OP.add))
            fl(v.scalar_tensor_tensor(out=q_w[:, :W], in0=q_u[:, :W], scalar=2.0,
                                      in1=q_n1[:, :W], op0=OP.is_ge, op1=OP.add))
            fl(v.scalar_tensor_tensor(out=q_u[:, :W], in0=q_w[:, :W], scalar=-2.0,
                                      in1=n_ap, op0=OP.mult, op1=OP.add))
            fl(v.tensor_single_scalar(out=q_n1[:, :W], in_=q_v[:, :W], scalar=0.5, op=OP.is_equal))
            fl(v.scalar_tensor_tensor(out=q_w[:, :W], in0=q_v[:, :W], scalar=1.5,
                                      in1=q_n1[:, :W], op0=OP.is_equal, op1=OP.add))
            fl(v.tensor_tensor(out=q_n1[:, :W], in0=q_w[:, :W], in1=q_u[:, :W], op=OP.mult))
            fl(v.tensor_tensor(out=q_v[:, :W], in0=n_ap, in1=q_n1[:, :W], op=OP.subtract))
            n_ap = q_v[:, :W]
        return v.tensor_tensor(out=dst_q, in0=n_ap, in1=sc, op=OP.mult)

    NW = 10

    with nc.Block() as block:
        esQ = contextlib.ExitStack()
        qt_sb = esQ.enter_context(nc.sbuf_tensor("qt_sb", [128, DCH * S], BF16))
        QTv = qt_sb[:, :]

        esA = contextlib.ExitStack()
        xnt_own = esA.enter_context(nc.sbuf_tensor("xnt_own", [128, DCH * TOK], BF16))
        xnt_oth = esA.enter_context(nc.sbuf_tensor("xnt_oth", [128, DCH * TOK], BF16))
        wqb = [esA.enter_context(nc.sbuf_tensor(f"wqb{i}", [128, DCH * 128], BF16))
               for i in range(2)]
        xbuf = [esA.enter_context(nc.sbuf_tensor(f"xbuf{i}", [128, D], F32))
                for i in range(2)]
        xnbuf = [esA.enter_context(nc.sbuf_tensor(f"xnbuf{i}", [128, D], BF16))
                 for i in range(2)]
        sqdump = esA.enter_context(nc.sbuf_tensor("sqdump", [128, D], BF16))
        p_q = [esA.enter_context(nc.psum_tensor(f"p_q{i}", [128, 512], F32))
               for i in range(2)]
        p_c = [esA.enter_context(nc.psum_tensor(f"p_c{i}", [128, 512], F32))
               for i in range(4)]

        @block.sync
        def _(sync):
            for dst, name in ((cpool, "cpool"), (cbf, "cbf"), (bqc, "bqc"),
                              (b1c, "b1c"), (b2c, "b2c"), (onesrow, "onesrow"),
                              (negrow, "negrow")):
                sync.dma_start(out=dst[:, :], in_=ap[name]).then_inc(sm["ldw"], 16)
            for i in range(NT_X):
                if i >= 2:
                    sync.wait_ge(sm["xn"], i - 1)
                sync.dma_start(out=xbuf[i % 2][:, :],
                               in_=x_d[i * 128:(i + 1) * 128, :]).then_inc(sm["ldx"][i % 2], 16)

        @block.gpsimd
        def _(g):
            for dc in range(DCH):
                if dc >= 2:
                    g.wait_ge(sm["mmq"], 8 * (dc - 1))
                src = wq_d[:, dc * 128:(dc + 1) * 128].rearrange(
                    "(hc p) d -> p hc d", p=128)
                g.dma_start(out=wqb[dc % 2][:, :].rearrange(
                    "p (hc d) -> p hc d", d=128), in_=src).then_inc(sm["gw"][dc % 2], 16)

        @block.scalar
        def _(s):
            s.wait_ge(sm["ldw"], 16 * 7)
            for i in range(NT_X):
                s.wait_ge(sm["ldx"][i % 2], 16 * (i // 2 + 1))
                if i >= 1:
                    s.wait_ge(sm["var"], i)
                s.activation(out=sqdump[:, :], in_=xbuf[i % 2][:, :],
                             func=AF.Square).then_inc(sm["sq"], 1)
                s.wait_ge(sm["var"], i + 1)
                s.activation(out=stats["sd"][:, i:i + 1], in_=stats["var"][:, i:i + 1],
                             func=AF.Sqrt, bias=cpool[:, C_EPS:C_EPS + 1]).then_inc(sm["sd"], 1)
                s.wait_ge(sm["xn"], i + 1)
                dst_t = xnt_own if i < 16 else xnt_oth
                trel = i % 16
                out3 = dst_t[:, :].rearrange("p (dc t) -> p dc t", t=TOK)[
                    :, :, trel * 128:(trel + 1) * 128]
                s.dma_start(out=out3, in_=xnbuf[i % 2][:, :],
                            transpose=True).then_inc(sm["tr"][i % 2], 16)

        @block.vector
        def _(v):
            v.wait_ge(sm["ldw"], 16 * 7)
            for i in range(NT_X):
                col = lambda t: stats[t][:, i:i + 1]
                v.wait_ge(sm["ldx"][i % 2], 16 * (i // 2 + 1))
                VF(v, v.reduce_sum(out=col("ssum"), in_=xbuf[i % 2][:, :],
                                   axis=mybir.AxisListType.X))
                v.wait_ge(sm["sq"], i + 1)
                VF(v, v.reduce_sum(out=col("ssq"), in_=sqdump[:, :],
                                   axis=mybir.AxisListType.X))
                VF(v, v.tensor_single_scalar(out=col("mu"), in_=col("ssum"),
                                             scalar=1.0 / D, op=OP.mult))
                VF(v, v.tensor_single_scalar(out=col("var"), in_=col("ssq"),
                                             scalar=1.0 / D, op=OP.mult))
                VF(v, v.tensor_tensor(out=col("t1"), in0=col("mu"), in1=col("mu"),
                                      op=OP.mult))
                VF(v, v.tensor_tensor(out=col("var"), in0=col("var"), in1=col("t1"),
                                      op=OP.subtract))
                v.sem_inc(sm["var"], 1)
                v.wait_ge(sm["sd"], i + 1)
                VF(v, v.reciprocal(out=col("r0"), in_=col("sd")))
                VF(v, v.tensor_single_scalar(out=col("t2"), in_=col("var"),
                                             scalar=cpool[:, C_EPS:C_EPS + 1], op=OP.add))
                VF(v, v.tensor_tensor(out=col("t1"), in0=col("r0"), in1=col("r0"), op=OP.mult))
                VF(v, v.tensor_tensor(out=col("t1"), in0=col("t2"), in1=col("t1"), op=OP.mult))
                VF(v, v.tensor_scalar(out=col("t1"), in0=col("t1"), scalar1=-0.5,
                                      scalar2=1.5, op0=OP.mult, op1=OP.add))
                VF(v, v.tensor_tensor(out=col("r0"), in0=col("r0"), in1=col("t1"), op=OP.mult))
                if i >= 2:
                    v.wait_ge(sm["tr"][i % 2], 16 * (i // 2))
                v.tensor_scalar(out=xnbuf[i % 2][:, :], in0=xbuf[i % 2][:, :],
                                scalar1=col("mu"), scalar2=col("r0"),
                                op0=OP.subtract, op1=OP.mult).then_inc(sm["xn"], 1)

        @block.tensor
        def _(t):
            t.wait_ge(sm["tr"][0], 16 * (NT_X // 2))
            t.wait_ge(sm["tr"][1], 16 * (NT_X // 2))
            for dc in range(DCH):
                t.wait_ge(sm["gw"][dc % 2], 16 * (dc // 2 + 1))
                for tt in range(8):
                    idx = dc * 8 + tt
                    if idx >= 2:
                        t.wait_ge(sm["qe"], idx - 1)
                    src = xnt_own if tt < 4 else xnt_oth
                    trel = tt % 4
                    for hc in range(DCH):
                        mm = t.matmul(p_q[idx % 2][:, :],
                                      lhsT=wqb[dc % 2][:, hc * 128:(hc + 1) * 128],
                                      rhs=src[:, hc * TOK + trel * 512: hc * TOK + trel * 512 + 512],
                                      start=(hc == 0), stop=(hc == DCH - 1))
                        if hc == DCH - 1:
                            mm.then_inc(sm["mmq"], 1)

        @block.scalar
        def _(s):
            for dc in range(DCH):
                for tt in range(8):
                    idx = dc * 8 + tt
                    s.wait_ge(sm["mmq"], idx + 1)
                    s.activation(out=QTv[:, dc * S + tt * 512: dc * S + tt * 512 + 512],
                                 in_=p_q[idx % 2][:, :], func=AF.Identity,
                                 bias=bqc[:, dc:dc + 1], scale=1.0).then_inc(sm["qe"], 1)

        @block.vector
        def _(v):
            for dc in range(DCH):
                v.wait_ge(sm["qe"], dc * 8 + 4)
                if dc >= 1:
                    v.wait_ge(sm["mmc"], dc)
                v.tensor_tensor(out=xnbuf[0][:, :],
                                in0=QTv[:, dc * S:dc * S + 1024],
                                in1=QTv[:, dc * S:dc * S + 1024], op=OP.mult)
                v.tensor_tensor(out=xnbuf[1][:, :],
                                in0=QTv[:, dc * S + 1024:dc * S + 2048],
                                in1=QTv[:, dc * S + 1024:dc * S + 2048],
                                op=OP.mult).then_inc(sm["sq2"], 1)
            v.wait_ge(sm["b2a"], 1)
            q_echain(v, "b2", 0, q_L[0][:, 0:8], W=8).then_inc(sm["b2b"], 1)
            v.wait_ge(sm["b2c2"], 1)
            q_round(v, "b2", 0, b2c[:, :], b2qc[:, :], W=8).then_inc(sm["b2d"], 1)

        @block.tensor
        def _(t):
            for dc in range(DCH):
                t.wait_ge(sm["sq2"], dc + 1)
                for qt4 in range(4):
                    buf = xnbuf[0] if qt4 < 2 else xnbuf[1]
                    off = (qt4 % 2) * 512
                    mm = t.matmul(p_c[qt4][0:1, :], lhsT=cbf[:, 0:1],
                                  rhs=buf[:, off:off + 512],
                                  start=(dc == 0), stop=(dc == DCH - 1))
                    if qt4 == 3:
                        mm.then_inc(sm["mmc"], 1)

        @block.scalar
        def _(s):
            s.wait_ge(sm["mmc"], DCH)
            for qt4 in range(4):
                s.copy(out=c_row[0:1, qt4 * 512:(qt4 + 1) * 512],
                       in_=p_c[qt4][0:1, :]).then_inc(sm["ce"], 1)
            s.activation(out=q_abs[:, 0:8], in_=b2c[:, :], func=AF.Abs)
            s.activation(out=q_L[0][:, 0:8], in_=q_abs[:, 0:8], func=AF.Ln,
                         bias=cpool[:, C_TINY:C_TINY + 1]).then_inc(sm["b2a"], 1)
            s.wait_ge(sm["b2b"], 1)
            q_scales(s, "b2", 0, W=8, want_sc=False).then_inc(sm["b2c2"], 1)

        @block.sync
        def _(sync):
            sync.wait_ge(sm["qe"], 64)
            sync.dma_start(out=sp_xnt, in_=xnt_own[:, :]).then_inc(sm["spx"], 16)
            if debug:
                for di, nm in enumerate(("ssum", "ssq", "mu", "var", "sd", "r0")):
                    sync.dma_start(out=ap["dbg_stats"][di], in_=stats[nm][:, :]
                                   ).then_inc(sm["spx"], 16)

        esA.close()

        esV = contextlib.ExitStack()
        v_sb = esV.enter_context(nc.sbuf_tensor("v_sb", [128, HCH * 1024], BF16))
        esB = contextlib.ExitStack()
        e_sb = esB.enter_context(nc.sbuf_tensor("e_sb", [128, HCH * 256], BF16))
        w2f = [esB.enter_context(nc.sbuf_tensor(f"w2f{i}", [128, D], F32))
               for i in range(2)]
        w2qo = [esB.enter_context(nc.sbuf_tensor(f"w2qo{i}", [128, D], FP8))
                for i in range(2)]
        s_out = esB.enter_context(nc.sbuf_tensor("s_out", [128, D], F32))
        p_sc = [esB.enter_context(nc.psum_tensor(f"p_sc{i}", [128, 512], F32))
                for i in range(3)]
        p_z = [esB.enter_context(nc.psum_tensor(f"p_z{i}", [128, 512], F32))
               for i in range(2)]
        p_pv = [esB.enter_context(nc.psum_tensor(f"p_pv{i}", [128, 512], F32))
                for i in range(2)]

        @block.scalar
        def _(s):
            s.wait_ge(sm["spx"], 16)
            s.wait_ge(sm["mmc"], DCH)
            for dc in range(DCH):
                out3 = v_sb[:, :].rearrange("p (kc d) -> p kc d", d=1024)[
                    :, :, dc * 128:(dc + 1) * 128]
                s.dma_start(out=out3, in_=QTv[:, dc * S:(dc + 1) * S],
                            transpose=True).then_inc(sm["tr2"], 16)
            s.wait_ge(sm["b2d"], 1)
            with nc.allow_non_contiguous_dma(reason="1KB one-off b2q gather"):
                s.dma_start(out=sp_b2.rearrange("(c p) -> p c", p=128),
                            in_=b2qc[:, :],
                            allow_slow_non_contiguous=True).then_inc(sm["b2r"], 16)
            s.wait_ge(sm["b2r"], 16)
            s.dma_start(out=b2q_row[:, :], in_=sp_b2.rearrange("(a d) -> a d", a=1)).then_inc(sm["b2r"], 16)

        @block.sync
        def _(sync):
            for qt8 in range(NQ8):
                for qc2 in range(2):
                    idx2 = qt8 * 2 + qc2
                    sync.wait_ge(sm["se"], idx2 * 2 + 2)
                    sync.dma_start(out=sp_s[idx2 * 128:(idx2 + 1) * 128, :],
                                   in_=s_out[:, :]).then_inc(sm["ss"], 16)
                for hc in range(qt8 * 4, qt8 * 4 + 4):
                    if hc >= 2:
                        sync.wait_ge(sm["w2v2"], 2 * (hc - 1))
                    sync.dma_start(out=w2f[hc % 2][:, :],
                                   in_=w2t_d[hc * 128:(hc + 1) * 128, :]
                                   ).then_inc(sm["w2l"][hc % 2], 16)
                    if hc >= 1:
                        sync.wait_ge(sm["w2v2"], 2 * hc)
                        sync.dma_start(out=sp_w2q[(hc - 1) * 128:hc * 128, :],
                                       in_=w2qo[(hc - 1) % 2][:, :]
                                       ).then_inc(sm["w2st"][(hc - 1) % 2], 16)
            sync.wait_ge(sm["w2v2"], 64)
            sync.dma_start(out=sp_w2q[31 * 128:32 * 128, :],
                           in_=w2qo[31 % 2][:, :]).then_inc(sm["w2st"][1], 16)

        @block.tensor
        def _(t):
            t.wait_ge(sm["ce"], 4)
            t.wait_ge(sm["b2r"], 32)
            for qt8 in range(NQ8):
                qoff = qt8 * 256
                for kc in range(HCH):
                    g = qt8 * HCH + kc
                    if g >= 2:
                        t.wait_ge(sm["exp"], g - 2)
                    pb = p_sc[g % 3]
                    for dc in range(DCH):
                        t.matmul(pb[:, 0:256],
                                 lhsT=QTv[:, dc * S + kc * 128: dc * S + kc * 128 + 128],
                                 rhs=QTv[:, dc * S + qoff: dc * S + qoff + 256],
                                 start=(dc == 0), stop=False)
                    t.matmul(pb[:, 0:256], lhsT=negrow[0:1, :],
                             rhs=c_row[0:1, qoff:qoff + 256],
                             start=False, stop=True).then_inc(sm["mms"], 1)
                for kc in range(HCH):
                    t.wait_ge(sm["exp"], qt8 * HCH + kc + 1)
                    if kc == 0 and qt8 >= 2:
                        t.wait_ge(sm["ze"], qt8 - 1)
                    mm = t.matmul(p_z[qt8 % 2][0:1, 0:256], lhsT=cbf[:, 0:1],
                                  rhs=e_sb[:, kc * 256:(kc + 1) * 256],
                                  start=(kc == 0), stop=(kc == HCH - 1))
                    if kc == HCH - 1:
                        mm.then_inc(sm["mmz"], 1)
                if qt8 == 0:
                    t.wait_ge(sm["tr2"], 16 * DCH)
                for qc2 in range(2):
                    for dt in range(2):
                        pvidx = qt8 * 4 + qc2 * 2 + dt
                        if pvidx >= 2:
                            t.wait_ge(sm["se"], pvidx - 1)
                        for kc in range(HCH):
                            mm = t.matmul(p_pv[pvidx % 2][:, :],
                                          lhsT=e_sb[:, kc * 256 + qc2 * 128: kc * 256 + qc2 * 128 + 128],
                                          rhs=v_sb[:, kc * 1024 + dt * 512: kc * 1024 + dt * 512 + 512],
                                          start=(kc == 0), stop=(kc == HCH - 1))
                            if kc == HCH - 1:
                                mm.then_inc(sm["mmpv"], 1)

        @block.scalar
        def _(s):
            for qt8 in range(NQ8):
                for kc in range(HCH):
                    g = qt8 * HCH + kc
                    s.wait_ge(sm["mms"], g + 1)
                    s.activation(out=e_sb[:, kc * 256:(kc + 1) * 256],
                                 in_=p_sc[g % 3][:, 0:256], func=AF.Exp,
                                 scale=SCALE).then_inc(sm["exp"], 1)
                s.wait_ge(sm["mmz"], qt8 + 1)
                if qt8 >= 1:
                    s.wait_ge(sm["zt"], 32 * qt8 - 16)
                s.copy(out=z_sb[0:1, :],
                       in_=p_z[qt8 % 2][0:1, 0:256]).then_inc(sm["ze"], 1)
                zdst = sp_z[qt8:qt8 + 1, :].rearrange("a (p c) -> (a c) p", c=2)
                with nc.allow_non_contiguous_dma(reason="1KB z row->col gather"):
                    s.dma_start(out=zdst, in_=z_sb[0:1, :],
                                allow_slow_non_contiguous=True).then_inc(sm["zt"], 16)
                s.wait_ge(sm["zt"], 32 * qt8 + 16)
                s.dma_start(out=zcols[:, qt8 * 2:qt8 * 2 + 2],
                            in_=sp_z[qt8:qt8 + 1, :].rearrange("a (p c) -> (a p) c", c=2)
                            ).then_inc(sm["zt"], 16)
                for qc2 in range(2):
                    idx2 = qt8 * 2 + qc2
                    for dt in range(2):
                        pvidx = idx2 * 2 + dt
                        s.wait_ge(sm["mmpv"], pvidx + 1)
                        s.wait_ge(sm["zr"], qt8 + 1)
                        if idx2 >= 1 and dt == 0:
                            s.wait_ge(sm["ss"], 16 * idx2)
                        s.activation(out=s_out[:, dt * 512:(dt + 1) * 512],
                                     in_=p_pv[pvidx % 2][:, :], func=AF.Identity,
                                     scale=invz[:, idx2:idx2 + 1],
                                     bias=0.0).then_inc(sm["se"], 1)
                for j in range(qt8 * 8, qt8 * 8 + 8):
                    hc, half = j // 2, j % 2
                    s.wait_ge(sm["w2l"][hc % 2], 16 * (hc // 2 + 1))
                    if j >= 2:
                        s.wait_ge(sm["w2v1"], j - 1)
                    srcw = w2f[hc % 2][:, half * 512:(half + 1) * 512]
                    s.activation(out=q_abs[:, :], in_=srcw, func=AF.Abs)
                    s.activation(out=q_L[j % 2][:, :], in_=q_abs[:, :], func=AF.Ln,
                                 bias=cpool[:, C_TINY:C_TINY + 1]).then_inc(sm["w2sa"], 1)
                    s.wait_ge(sm["w2v1"], j + 1)
                    if j >= 2:
                        s.wait_ge(sm["w2v2"], j - 1)
                    q_scales(s, "w2", j, want_sc=False).then_inc(sm["w2sb"], 1)

        @block.vector
        def _(v):
            for qt8 in range(NQ8):
                v.wait_ge(sm["zt"], 32 * (qt8 + 1))
                v.reciprocal(out=invz[:, qt8 * 2:qt8 * 2 + 2],
                             in_=zcols[:, qt8 * 2:qt8 * 2 + 2]).then_inc(sm["zr"], 1)
                for j in range(qt8 * 8, qt8 * 8 + 8):
                    hc, half = j // 2, j % 2
                    v.wait_ge(sm["w2sa"], j + 1)
                    q_echain(v, "w2", j, q_L[j % 2][:, :]).then_inc(sm["w2v1"], 1)
                    v.wait_ge(sm["w2sb"], j + 1)
                    if half == 0 and hc >= 2:
                        v.wait_ge(sm["w2st"][hc % 2], 16 * (hc // 2))
                    srcw = w2f[hc % 2][:, half * 512:(half + 1) * 512]
                    dstw = w2qo[hc % 2][:, half * 512:(half + 1) * 512]
                    q_round(v, "w2", j, srcw, dstw).then_inc(sm["w2v2"], 1)

        esB.close()
        esV.close()
        esQ.close()

        esC = contextlib.ExitStack()
        qb["u"] = esC.enter_context(nc.sbuf_tensor("qb_u", [128, 512], BF16))
        qb["ef0"] = esC.enter_context(nc.sbuf_tensor("qb_ef0", [128, 512], BF16))
        qb["w"] = esC.enter_context(nc.sbuf_tensor("qb_w", [128, 512], BF16))
        qb["n1"] = esC.enter_context(nc.sbuf_tensor("qb_n1", [128, 512], BF16))
        qb["e2"] = [esC.enter_context(nc.sbuf_tensor(f"qb_e2_{i}", [128, 512], BF16))
                    for i in range(2)]
        xnt2 = esC.enter_context(nc.sbuf_tensor("xnt2", [128, DCH * TOK], BF16))
        w1b = [esC.enter_context(nc.sbuf_tensor(f"w1b{i}", [128, DCH * 256], BF16))
               for i in range(2)]
        w2q_sb = esC.enter_context(nc.sbuf_tensor("w2q_sb", [128, HCH * D], FP8))
        hq_sb = esC.enter_context(nc.sbuf_tensor("hq_sb", [128, HCH * 512], FP8))
        hbuf = [esC.enter_context(nc.sbuf_tensor(f"hbuf{i}", [128, 512], F32))
                for i in range(2)]
        fo32 = [esC.enter_context(nc.sbuf_tensor(f"fo32_{i}", [128, 512], F32))
                for i in range(2)]
        oL = [esC.enter_context(nc.sbuf_tensor(f"oL{i}", [128, 512], F32))
              for i in range(2)]
        oe2 = [esC.enter_context(nc.sbuf_tensor(f"oe2_{i}", [128, 512], F32))
               for i in range(2)]
        osi = [esC.enter_context(nc.sbuf_tensor(f"osi{i}", [128, 512], F32))
               for i in range(2)]
        osc = [esC.enter_context(nc.sbuf_tensor(f"osc{i}", [128, 512], F32))
               for i in range(2)]
        oq = [esC.enter_context(nc.sbuf_tensor(f"oq{i}", [128, 512], F32))
              for i in range(2)]
        gbuf = [esC.enter_context(nc.sbuf_tensor(f"gbuf{i}", [128, 512], F32))
                for i in range(2)]
        stile = [esC.enter_context(nc.sbuf_tensor(f"stile{i}", [128, 512], F32))
                 for i in range(2)]
        xtile = [esC.enter_context(nc.sbuf_tensor(f"xtile{i}", [128, 512], F32))
                 for i in range(2)]
        otile = [esC.enter_context(nc.sbuf_tensor(f"otile{i}", [128, 512], F32))
                 for i in range(2)]
        p_h = [esC.enter_context(nc.psum_tensor(f"p_h{i}", [128, 512], F32))
               for i in range(2)]
        p_f = [esC.enter_context(nc.psum_tensor(f"p_f{i}", [128, 512], F32))
               for i in range(2)]

        def c_gate(eng):
            eng.wait_ge(sm["mmpv"], 4 * NQ8)
            eng.wait_ge(sm["w2st"][0], 16 * (HCH // 2))
            eng.wait_ge(sm["w2st"][1], 16 * (HCH // 2))
            eng.wait_ge(sm["ss"], 16 * 2 * NQ8)

        @block.sync
        def _(sync):
            c_gate(sync)
            sync.dma_start(out=xnt2[:, :], in_=sp_xnt).then_inc(sm["cxnt"], 16)
            sync.dma_start(out=w2q_sb[:, :].rearrange("p (hc d) -> p hc d", d=D),
                           in_=sp_w2q.rearrange("(hc p) d -> p hc d", p=128)
                           ).then_inc(sm["cw2"], 16)
            for K in range(32):
                tc, dt = K // 2, K % 2
                if K >= 2:
                    sync.wait_ge(sm["cmb"], K - 1)
                sync.dma_start(out=stile[K % 2][:, :],
                               in_=sp_s[tc * 128:(tc + 1) * 128, dt * 512:dt * 512 + 512]
                               ).then_inc(sm["cs"][K % 2], 16)
                sync.dma_start(out=xtile[K % 2][:, :],
                               in_=x_d[tc * 128:(tc + 1) * 128, dt * 512:dt * 512 + 512]
                               ).then_inc(sm["cx"][K % 2], 16)
                sync.wait_ge(sm["cmb"], K + 1)
                sync.dma_start(out=out_d[tc * 128:(tc + 1) * 128, dt * 512:dt * 512 + 512],
                               in_=otile[K % 2][:, :]).then_inc(sm["outd"][K % 2], 16)
            sync.wait_ge(sm["outd"][0], 16 * 16)
            sync.wait_ge(sm["outd"][1], 16 * 16)

        @block.gpsimd
        def _(g):
            c_gate(g)
            for stream in range(4 * 16):
                if stream >= 2:
                    g.wait_ge(sm["mmh"], 2 * (stream - 1))
                hb = stream % 16
                src = w1t_d[:, hb * 256:(hb + 1) * 256].rearrange(
                    "(dc p) h -> p dc h", p=128)
                g.dma_start(out=w1b[stream % 2][:, :].rearrange(
                    "p (dc h) -> p dc h", h=256), in_=src).then_inc(sm["cw1"][stream % 2], 16)

        @block.tensor
        def _(t):
            t.wait_ge(sm["cxnt"], 16)
            for tt in range(4):
                for hb in range(16):
                    stream = tt * 16 + hb
                    t.wait_ge(sm["cw1"][stream % 2], 16 * (stream // 2 + 1))
                    for hc2 in range(2):
                        j = tt * 32 + hb * 2 + hc2
                        if j >= 2:
                            t.wait_ge(sm["hr"], j - 1)
                        for dc in range(DCH):
                            mm = t.matmul(p_h[j % 2][:, :],
                                          lhsT=w1b[stream % 2][:, dc * 256 + hc2 * 128: dc * 256 + hc2 * 128 + 128],
                                          rhs=xnt2[:, dc * TOK + tt * 512: dc * TOK + tt * 512 + 512],
                                          start=(dc == 0), stop=(dc == DCH - 1))
                            if dc == DCH - 1:
                                mm.then_inc(sm["mmh"], 1)
                if tt == 0:
                    t.wait_ge(sm["cw2"], 16)
                t.wait_ge(sm["hq"], tt * 32 + 32)
                for tc2 in range(4):
                    for dt in range(2):
                        K = tt * 8 + tc2 * 2 + dt
                        if K >= 2:
                            t.wait_ge(sm["fe"], K - 1)
                        for hc in range(HCH):
                            t.matmul(p_f[K % 2][:, :],
                                     lhsT=hq_sb[:, hc * 512 + tc2 * 128: hc * 512 + tc2 * 128 + 128],
                                     rhs=w2q_sb[:, hc * D + dt * 512: hc * D + dt * 512 + 512],
                                     start=(hc == 0), stop=False)
                        t.matmul(p_f[K % 2][:, :], lhsT=onesrow[0:1, :],
                                 rhs=b2q_row[0:1, dt * 512:dt * 512 + 512],
                                 start=False, stop=True).then_inc(sm["mmf"], 1)

        @block.scalar
        def _(s):
            c_gate(s)
            for tt in range(4):
                for j in range(tt * 32, tt * 32 + 32):
                    hcl = j - tt * 32
                    s.wait_ge(sm["mmh"], j + 1)
                    if j >= 2:
                        s.wait_ge(sm["hq"], j - 1)
                        s.wait_ge(sm["he"], j - 1)
                    s.activation(out=hbuf[j % 2][:, :], in_=p_h[j % 2][:, :],
                                 func=AF.Relu, bias=b1c[:, hcl:hcl + 1]).then_inc(sm["hr"], 1)
                    s.activation(out=q_L[j % 2][:, :], in_=hbuf[j % 2][:, :],
                                 func=AF.Ln,
                                 bias=cpool[:, C_TINY:C_TINY + 1]).then_inc(sm["hl"], 1)
                    if j > tt * 32:
                        s.wait_ge(sm["he"], j)
                        if j >= 2:
                            s.wait_ge(sm["hq"], j - 1)
                        q_scales(s, "h", j - 1).then_inc(sm["hs"], 1)
                s.wait_ge(sm["he"], tt * 32 + 32)
                q_scales(s, "h", tt * 32 + 31).then_inc(sm["hs"], 1)
                for K in range(tt * 8, tt * 8 + 8):
                    s.wait_ge(sm["mmf"], K + 1)
                    if K >= 2:
                        s.wait_ge(sm["oq"], K - 1)
                    s.copy(out=fo32[K % 2][:, :], in_=p_f[K % 2][:, :]).then_inc(sm["fe"], 1)
                    s.activation(out=q_abs[:, :], in_=fo32[K % 2][:, :], func=AF.Abs)
                    s.activation(out=oL[K % 2][:, :], in_=q_abs[:, :], func=AF.Ln,
                                 bias=cpool[:, C_TINY:C_TINY + 1]).then_inc(sm["oa"], 1)
                    s.wait_ge(sm["oe"], K + 1)
                    s.activation(out=osi[K % 2][:, :], in_=oe2[K % 2][:, :],
                                 func=AF.Exp, scale=-LN2,
                                 bias=cpool[:, C_M10P:C_M10P + 1])
                    s.activation(out=osc[K % 2][:, :], in_=oe2[K % 2][:, :],
                                 func=AF.Exp, scale=LN2,
                                 bias=cpool[:, C_M10N:C_M10N + 1]).then_inc(sm["os"], 1)
                    s.wait_ge(sm["oq"], K + 1)
                    s.activation(out=gbuf[K % 2][:, :], in_=oq[K % 2][:, :],
                                 func=AF.Silu).then_inc(sm["g"], 1)

        @block.vector
        def _(v):
            c_gate(v)
            for tt in range(4):
                for j in range(tt * 32, tt * 32 + 32):
                    v.wait_ge(sm["hl"], j + 1)
                    q_echain(v, "h", j, q_L[j % 2][:, :]).then_inc(sm["he"], 1)
                    if j > tt * 32:
                        v.wait_ge(sm["hs"], j)
                        jj = j - 1
                        hcp = jj - tt * 32
                        q_round(v, "h", jj, hbuf[jj % 2][:, :],
                                hq_sb[:, hcp * 512:(hcp + 1) * 512]).then_inc(sm["hq"], 1)
                v.wait_ge(sm["hs"], tt * 32 + 32)
                jj = tt * 32 + 31
                q_round(v, "h", jj, hbuf[jj % 2][:, :],
                        hq_sb[:, 31 * 512:32 * 512]).then_inc(sm["hq"], 1)
                for K in range(tt * 8, tt * 8 + 8):
                    W = 512
                    v.wait_ge(sm["oa"], K + 1)
                    if K >= 2:
                        v.wait_ge(sm["os"], K - 1)
                    v.tensor_scalar(out=q_u[:, :W], in0=oL[K % 2][:, :], scalar1=INVLN2,
                                    scalar2=cpool[:, C_CL10:C_CL10 + 1],
                                    op0=OP.mult, op1=OP.max)
                    v.tensor_single_scalar(out=q_ei[:, :W], in_=q_u[:, :W], scalar=0.0, op=OP.add)
                    v.tensor_single_scalar(out=q_cm1[:, :W], in_=q_ei[:, :W], scalar=-1.0, op=OP.add)
                    v.tensor_tensor(out=q_w[:, :W], in0=q_u[:, :W], in1=q_cm1[:, :W], op=OP.subtract)
                    v.scalar_tensor_tensor(out=q_n1[:, :W], in0=q_w[:, :W], scalar=1.0,
                                           in1=q_cm1[:, :W], op0=OP.is_ge, op1=OP.add)
                    v.scalar_tensor_tensor(out=oe2[K % 2][:, :], in0=q_w[:, :W], scalar=2.0,
                                           in1=q_n1[:, :W], op0=OP.is_ge, op1=OP.add).then_inc(sm["oe"], 1)
                    v.wait_ge(sm["os"], K + 1)
                    v.tensor_tensor(out=q_t[:, :W], in0=fo32[K % 2][:, :],
                                    in1=osi[K % 2][:, :], op=OP.mult)
                    v.tensor_single_scalar(out=q_ei[:, :W], in_=q_t[:, :W], scalar=0.0, op=OP.add)
                    v.tensor_single_scalar(out=q_cm1[:, :W], in_=q_ei[:, :W], scalar=-1.0, op=OP.add)
                    v.tensor_tensor(out=q_v[:, :W], in0=q_t[:, :W], in1=q_cm1[:, :W], op=OP.subtract)
                    v.scalar_tensor_tensor(out=q_n1[:, :W], in0=q_v[:, :W], scalar=0.5,
                                           in1=q_cm1[:, :W], op0=OP.is_ge, op1=OP.add)
                    v.scalar_tensor_tensor(out=q_w[:, :W], in0=q_v[:, :W], scalar=1.5,
                                           in1=q_n1[:, :W], op0=OP.is_ge, op1=OP.add)
                    v.tensor_tensor(out=oq[K % 2][:, :], in0=q_w[:, :W],
                                    in1=osc[K % 2][:, :], op=OP.mult).then_inc(sm["oq"], 1)
                    v.wait_ge(sm["g"], K + 1)
                    v.wait_ge(sm["cs"][K % 2], 16 * (K // 2 + 1))
                    v.wait_ge(sm["cx"][K % 2], 16 * (K // 2 + 1))
                    if K >= 2:
                        v.wait_ge(sm["outd"][K % 2], 16 * (K // 2))
                    v.tensor_tensor(out=otile[K % 2][:, :], in0=gbuf[K % 2][:, :],
                                    in1=stile[K % 2][:, :], op=OP.mult)
                    v.tensor_tensor(out=otile[K % 2][:, :], in0=otile[K % 2][:, :],
                                    in1=xtile[K % 2][:, :], op=OP.add).then_inc(sm["cmb"], 1)

        esC.close()
    es.close()
    return nc


_NC_CACHE = [None]


def kernel(L0_attn_inputs, L0_attn_Wq, L0_attn_bq, L0_ffn_w1, L0_ffn_b1,
           L0_ffn_w2, L0_ffn_b2):
    x = np.asarray(L0_attn_inputs, dtype=np.float32)
    wq = np.ascontiguousarray(np.asarray(L0_attn_Wq, dtype=np.float32))
    bq = np.asarray(L0_attn_bq, dtype=np.float32)
    w1 = np.asarray(L0_ffn_w1, dtype=np.float32)
    b1 = np.asarray(L0_ffn_b1, dtype=np.float32)
    w2 = np.asarray(L0_ffn_w2, dtype=np.float32)
    b2 = np.asarray(L0_ffn_b2, dtype=np.float32)

    w1t = np.ascontiguousarray(w1.T)
    w2t = np.ascontiguousarray(w2.T)
    bqc = np.ascontiguousarray(bq.reshape(DCH, 128).T)
    b1c = np.ascontiguousarray(b1.reshape(HCH, 128).T)
    b2c = np.ascontiguousarray(b2.reshape(8, 128).T)

    cpool = np.zeros((128, 8), dtype=np.float32)
    cpool[:, C_TINY] = 1e-38
    cpool[:, C_M3P] = 3 * LN2
    cpool[:, C_M10P] = 10 * LN2
    cpool[:, C_CL3] = -6.0
    cpool[:, C_CL10] = -14.0
    cpool[:, C_M3N] = -3 * LN2
    cpool[:, C_M10N] = -10 * LN2
    cpool[:, C_EPS] = 1e-5
    cbf = np.ones((128, 2), dtype=BF)
    onesrow = np.ones((1, 128), dtype=np.float32)
    negrow = np.full((1, 128), -1.0, dtype=np.float32)

    if _NC_CACHE[0] is None:
        _NC_CACHE[0] = build_nc()
    nc = _NC_CACHE[0]

    in_maps = []
    for c in range(8):
        b, hf = c // 2, c % 2
        xb = x[b]
        xc = np.ascontiguousarray(
            np.concatenate([xb[hf * TOK:(hf + 1) * TOK],
                            xb[(1 - hf) * TOK:(2 - hf) * TOK]], axis=0))
        in_maps.append({
            "x": xc, "wq": wq, "bqc": bqc, "w1t": w1t, "b1c": b1c,
            "w2t": w2t, "b2c": b2c, "cpool": cpool, "cbf": cbf,
            "onesrow": onesrow, "negrow": negrow,
        })
    res = run_bass_kernel_spmd(nc, in_maps, core_ids=list(range(8)))
    y = np.empty((B, S, D), dtype=np.float32)
    for c in range(8):
        b, hf = c // 2, c % 2
        y[b, hf * TOK:(hf + 1) * TOK] = np.asarray(res.results[c]["out"])
    return y
